# version 03
# baseline (speedup 1.0000x reference)
import numpy as np
import concourse.bass as bass
import concourse.mybir as mybir
from concourse.bass_utils import run_bass_kernel_spmd

F32 = mybir.dt.float32
BF16 = mybir.dt.bfloat16
I32 = mybir.dt.int32
AF = mybir.ActivationFunctionType
ALU = mybir.AluOpType
AX = mybir.AxisListType

E = 1024
EC = 8
NCORE = 8


class Ser:

    def __init__(self, nc):
        self.nc = nc
        self.ch = nc.alloc_semaphore("ch")
        self.cnt = 0

    def dma(self, dst, src):
        self.nc.sync.wait_ge(self.ch, self.cnt)
        self.nc.sync.dma_start(out=dst, in_=src).then_inc(self.ch, 16)
        self.cnt += 16

    def op(self, eng, inst):
        inst.then_inc(self.ch, 1)
        self.cnt += 1

    def run(self, eng, fn, *args, **kw):
        eng.wait_ge(self.ch, self.cnt)
        fn(*args, **kw).then_inc(self.ch, 1)
        self.cnt += 1


def build_nc(NSEQ, S):
    NCH = S // 128
    nc = bass.Bass("TRN2", target_bir_lowering=False)

    x_d = nc.dram_tensor("x", [NSEQ, S, E], F32, kind="ExternalInput").ap()
    wq_d = nc.dram_tensor("wq", [E, E], F32, kind="ExternalInput").ap()
    wk_d = nc.dram_tensor("wk", [E, E], F32, kind="ExternalInput").ap()
    wv_d = nc.dram_tensor("wv", [E, E], F32, kind="ExternalInput").ap()
    wf_d = nc.dram_tensor("wf", [E, E], I32, kind="ExternalInput").ap()
    ws_d = nc.dram_tensor("wsc", [1, 1], F32, kind="ExternalInput").ap()
    bias_d = nc.dram_tensor("bias", [1, E], F32, kind="ExternalInput").ap()
    out_d = nc.dram_tensor("out", [NSEQ, S, E], F32, kind="ExternalOutput").ap()

    ht_hi_d = nc.dram_tensor("ht_hi_s", [128, EC * E], BF16, kind="Internal").ap()
    ht_lo_d = nc.dram_tensor("ht_lo_s", [128, EC * E], BF16, kind="Internal").ap()
    wvt_hi_d = nc.dram_tensor("wvt_hi_s", [128, EC * E], BF16, kind="Internal").ap()
    wvt_lo_d = nc.dram_tensor("wvt_lo_s", [128, EC * E], BF16, kind="Internal").ap()
    wt_d = nc.dram_tensor("wt_s", [128, EC * E], BF16, kind="Internal").ap()

    ser = Ser(nc)
    V, SC, T, G = nc.vector, nc.scalar, nc.tensor, nc.gpsimd

    sb = lambda nm, shp, dt=F32: nc.alloc_sbuf_tensor(nm, shp, dt)
    ident_f = sb("ident_f", [128, 128]).ap()
    ident_b = sb("ident_b", [128, 128], BF16).ap()
    mask = sb("mask", [128, 128]).ap()
    ones_row = sb("ones_row", [1, 128]).ap()
    ws_sb = sb("ws_sb", [1, 1]).ap()
    bias_sb = sb("bias_sb", [1, E]).ap()
    ws_bc = sb("ws_bc", [128, 1]).ap()
    bias_bc = sb("bias_bc", [128, E]).ap()
    m_st = sb("m_st", [128, NCH]).ap()
    rs_st = sb("rs_st", [128, NCH]).ap()
    st = sb("st", [128, 24]).ap()
    c5 = sb("c5", [128, 1]).ap()
    c6 = sb("c6", [128, 1]).ap()
    xb = sb("xb", [128, E]).ap()
    A = sb("A", [128, E]).ap()
    lnq_hi = sb("lnq_hi", [128, E], BF16).ap()
    lnq_lo = sb("lnq_lo", [128, E], BF16).ap()

    ps0 = nc.alloc_psum_tensor("ps0", [128, 512], F32).ap()
    ps1 = nc.alloc_psum_tensor("ps1", [128, 512], F32).ap()
    ps2 = nc.alloc_psum_tensor("ps2", [128, 512], F32).ap()
    ps3 = nc.alloc_psum_tensor("ps3", [128, 512], F32).ap()
    ptf = nc.alloc_psum_tensor("ptf", [128, 512], F32).ap()
    ptb0 = nc.alloc_psum_tensor("ptb0", [128, 1024], BF16).ap()
    ptb1 = nc.alloc_psum_tensor("ptb1", [128, 1024], BF16).ap()

    def col(i):
        return st[:, i:i + 1]

    ser.run(G, G.memset, ones_row, 1.0)
    ser.run(G, G.memset, c5, 1e-5)
    ser.run(G, G.memset, c6, 1e-6)
    ser.run(G, G.memset, mask, 0.0)
    ser.run(G, G.memset, ident_f, 1.0)
    ser.run(G, lambda: G.affine_select(
        out=ident_f, in_=ident_f, compare_op=ALU.is_equal,
        fill=0.0, base=0, pattern=[[-1, 128]], channel_multiplier=1))
    ser.run(V, V.tensor_copy, ident_b, ident_f)
    ser.run(G, lambda: G.affine_select(
        out=mask, in_=mask, compare_op=ALU.is_ge,
        fill=-1e30, base=0, pattern=[[-1, 128]], channel_multiplier=1))

    ser.dma(ws_sb, ws_d)
    ser.dma(bias_sb, bias_d)
    ser.run(T, T.matmul, ps0[:, 0:1], ones_row, ws_sb, start=True, stop=True)
    ser.run(V, V.tensor_copy, ws_bc, ps0[:, 0:1])
    for h in range(2):
        ser.run(T, T.matmul, ps0[:, 0:512], ones_row, bias_sb[0:1, h * 512:(h + 1) * 512],
                start=True, stop=True)
        ser.run(V, V.tensor_copy, bias_bc[:, h * 512:(h + 1) * 512], ps0[:, 0:512])

    if True:
        with (nc.sbuf_tensor("stg_i32", [128, E], I32) as stg_i32_h,
              nc.sbuf_tensor("stg_bf", [128, E], BF16) as stg_bf_h,
              nc.sbuf_tensor("wt_sb", [128, EC * E], BF16) as wt_sb_h):
            stg_i32, stg_bf, wt_sb = stg_i32_h.ap(), stg_bf_h.ap(), wt_sb_h.ap()
            for i in range(EC):
                ser.dma(stg_i32, wf_d[i * 128:(i + 1) * 128, :])
                ser.run(V, V.tensor_copy, stg_bf, stg_i32)
                for j in range(EC):
                    ser.run(T, T.transpose, ptb0[:, j * 128:(j + 1) * 128],
                            stg_bf[:, j * 128:(j + 1) * 128], ident_b)
                for j in range(EC):
                    ser.run(V, V.tensor_copy, wt_sb[:, j * E + i * 128: j * E + (i + 1) * 128],
                            ptb0[:, j * 128:(j + 1) * 128])
            ser.dma(wt_d, wt_sb)

        with (nc.sbuf_tensor("wqs_hi", [128, EC * E], BF16) as wqs_hi_h,
              nc.sbuf_tensor("wqs_lo", [128, EC * E], BF16) as wqs_lo_h,
              nc.sbuf_tensor("wks_hi", [128, EC * E], BF16) as wks_hi_h,
              nc.sbuf_tensor("wks_lo", [128, EC * E], BF16) as wks_lo_h,
              nc.sbuf_tensor("stg_f", [128, E], F32) as stg_f_h,
              nc.sbuf_tensor("ht_hi", [128, EC * E], BF16) as ht_hi_h,
              nc.sbuf_tensor("ht_lo", [128, EC * E], BF16) as ht_lo_h):
            wqs_hi, wqs_lo = wqs_hi_h.ap(), wqs_lo_h.ap()
            wks_hi, wks_lo = wks_hi_h.ap(), wks_lo_h.ap()
            stg_f = stg_f_h.ap()
            ht_hi, ht_lo = ht_hi_h.ap(), ht_lo_h.ap()
            for d in range(EC):
                ser.dma(stg_f, wq_d[d * 128:(d + 1) * 128, :])
                ser.run(SC, SC.copy, wqs_hi[:, d * E:(d + 1) * E], stg_f)
                ser.run(V, V.tensor_tensor, wqs_lo[:, d * E:(d + 1) * E], stg_f,
                        wqs_hi[:, d * E:(d + 1) * E], ALU.subtract)
                ser.dma(stg_f, wk_d[d * 128:(d + 1) * 128, :])
                ser.run(SC, SC.copy, wks_hi[:, d * E:(d + 1) * E], stg_f)
                ser.run(V, V.tensor_tensor, wks_lo[:, d * E:(d + 1) * E], stg_f,
                        wks_hi[:, d * E:(d + 1) * E], ALU.subtract)
            for f in range(EC):
                for h in range(2):
                    ps = ps0 if h == 0 else ps1
                    for d in range(EC):
                        lw_hi = wks_hi[:, d * E + f * 128: d * E + (f + 1) * 128]
                        lw_lo = wks_lo[:, d * E + f * 128: d * E + (f + 1) * 128]
                        rh_hi = wqs_hi[:, d * E + h * 512: d * E + h * 512 + 512]
                        rh_lo = wqs_lo[:, d * E + h * 512: d * E + h * 512 + 512]
                        ser.run(T, T.matmul, ps, lw_hi, rh_hi, start=(d == 0), stop=False)
                        ser.run(T, T.matmul, ps, lw_hi, rh_lo, start=False, stop=False)
                        ser.run(T, T.matmul, ps, lw_lo, rh_hi, start=False,
                                stop=(d == EC - 1))
                    dst_hi = ht_hi[:, f * E + h * 512: f * E + h * 512 + 512]
                    dst_lo = ht_lo[:, f * E + h * 512: f * E + h * 512 + 512]
                    ser.run(V, V.tensor_scalar_mul, dst_hi, ps, 0.03125)
                    ser.run(V, lambda dst_lo=dst_lo, ps=ps, dst_hi=dst_hi:
                            V.scalar_tensor_tensor(dst_lo, ps, 0.03125, dst_hi,
                                                   ALU.mult, ALU.subtract))
            ser.dma(ht_hi_d, ht_hi)
            ser.dma(ht_lo_d, ht_lo)

        with (nc.sbuf_tensor("stg_f2", [128, E], F32) as stg_f2_h,
              nc.sbuf_tensor("stg_h", [128, E], BF16) as stg_h_h,
              nc.sbuf_tensor("stg_l", [128, E], BF16) as stg_l_h,
              nc.sbuf_tensor("wvt_hi", [128, EC * E], BF16) as wvt_hi_h,
              nc.sbuf_tensor("wvt_lo", [128, EC * E], BF16) as wvt_lo_h):
            stg_f2, stg_h, stg_l = stg_f2_h.ap(), stg_h_h.ap(), stg_l_h.ap()
            wvt_hi, wvt_lo = wvt_hi_h.ap(), wvt_lo_h.ap()
            for i in range(EC):
                ser.dma(stg_f2, wv_d[i * 128:(i + 1) * 128, :])
                ser.run(SC, SC.copy, stg_h, stg_f2)
                ser.run(V, V.tensor_tensor, stg_l, stg_f2, stg_h, ALU.subtract)
                for j in range(EC):
                    ser.run(T, T.transpose, ptb0[:, j * 128:(j + 1) * 128],
                            stg_h[:, j * 128:(j + 1) * 128], ident_b)
                    ser.run(T, T.transpose, ptb1[:, j * 128:(j + 1) * 128],
                            stg_l[:, j * 128:(j + 1) * 128], ident_b)
                for j in range(EC):
                    ser.run(V, V.tensor_copy,
                            wvt_hi[:, j * E + i * 128: j * E + (i + 1) * 128],
                            ptb0[:, j * 128:(j + 1) * 128])
                    ser.run(V, V.tensor_copy,
                            wvt_lo[:, j * E + i * 128: j * E + (i + 1) * 128],
                            ptb1[:, j * 128:(j + 1) * 128])
            ser.dma(wvt_hi_d, wvt_hi)
            ser.dma(wvt_lo_d, wvt_lo)

    kgt_hi = sb("kgt_hi", [128, EC * S], BF16).ap()
    kgt_lo = sb("kgt_lo", [128, EC * S], BF16).ap()
    v_hi = sb("v_hi", [128, NCH * E], BF16).ap()
    v_lo = sb("v_lo", [128, NCH * E], BF16).ap()

    def ln1_chunk(s, c, compute_stats):
        ser.dma(xb, x_d[s, c * 128:(c + 1) * 128, :])
        if compute_stats:
            ser.run(SC, SC.activation, A, xb, AF.Square, accum_out=col(0))
            ser.run(V, V.reduce_sum, col(1), xb, axis=AX.X)
            ser.run(V, V.tensor_scalar_mul, m_st[:, c:c + 1], col(1), 1.0 / E)
            ser.run(V, V.tensor_scalar_mul, col(2), col(0), 1.0 / E)
            ser.run(V, V.tensor_tensor, col(3), m_st[:, c:c + 1], m_st[:, c:c + 1],
                    ALU.mult)
            ser.run(V, V.tensor_tensor, col(2), col(2), col(3), ALU.subtract)
            ser.run(SC, SC.activation, col(4), col(2), AF.Ln, bias=c5)
            ser.run(SC, SC.activation, rs_st[:, c:c + 1], col(4), AF.Exp, scale=-0.5)
        ser.run(V, V.tensor_scalar, A, xb, m_st[:, c:c + 1], rs_st[:, c:c + 1],
                ALU.subtract, ALU.mult)
        ser.run(SC, SC.copy, lnq_hi, A)
        ser.run(V, V.tensor_tensor, lnq_lo, A, lnq_hi, ALU.subtract)

    def transpose_hilo(dst_hi, dst_lo, tcol):
        for j in range(EC):
            ser.run(T, T.transpose, ptb0[:, j * 128:(j + 1) * 128],
                    lnq_hi[:, j * 128:(j + 1) * 128], ident_b)
            ser.run(T, T.transpose, ptb1[:, j * 128:(j + 1) * 128],
                    lnq_lo[:, j * 128:(j + 1) * 128], ident_b)
        for j in range(EC):
            ser.run(V, V.tensor_copy,
                    dst_hi[:, j * 512 + tcol * 128: j * 512 + (tcol + 1) * 128],
                    ptb0[:, j * 128:(j + 1) * 128])
            ser.run(V, V.tensor_copy,
                    dst_lo[:, j * 512 + tcol * 128: j * 512 + (tcol + 1) * 128],
                    ptb1[:, j * 128:(j + 1) * 128])

    for s in range(NSEQ):
        with (nc.sbuf_tensor(f"ht_hi_p{s}", [128, EC * E], BF16) as ht_hi_p_h,
              nc.sbuf_tensor(f"ht_lo_p{s}", [128, EC * E], BF16) as ht_lo_p_h,
              nc.sbuf_tensor(f"lnt_hi{s}", [128, EC * 512], BF16) as lnt_hi_h,
              nc.sbuf_tensor(f"lnt_lo{s}", [128, EC * 512], BF16) as lnt_lo_h):
            ht_hi_p, ht_lo_p = ht_hi_p_h.ap(), ht_lo_p_h.ap()
            lnt_hi, lnt_lo = lnt_hi_h.ap(), lnt_lo_h.ap()
            ser.dma(ht_hi_p, ht_hi_d)
            ser.dma(ht_lo_p, ht_lo_d)
            for g in range(NCH // 4):
                for tc in range(4):
                    ln1_chunk(s, g * 4 + tc, compute_stats=True)
                    transpose_hilo(lnt_hi, lnt_lo, tc)
                for e in range(EC):
                    ps = ps0 if e % 2 == 0 else ps1
                    for f in range(EC):
                        lw_hi = ht_hi_p[:, f * E + e * 128: f * E + (e + 1) * 128]
                        lw_lo = ht_lo_p[:, f * E + e * 128: f * E + (e + 1) * 128]
                        rh_hi = lnt_hi[:, f * 512:(f + 1) * 512]
                        rh_lo = lnt_lo[:, f * 512:(f + 1) * 512]
                        ser.run(T, T.matmul, ps, lw_hi, rh_hi, start=(f == 0), stop=False)
                        ser.run(T, T.matmul, ps, lw_hi, rh_lo, start=False, stop=False)
                        ser.run(T, T.matmul, ps, lw_lo, rh_hi, start=False,
                                stop=(f == EC - 1))
                    dst_hi = kgt_hi[:, e * S + g * 512: e * S + (g + 1) * 512]
                    dst_lo = kgt_lo[:, e * S + g * 512: e * S + (g + 1) * 512]
                    ser.run(V, V.tensor_copy, dst_hi, ps)
                    ser.run(V, lambda dst_lo=dst_lo, ps=ps, dst_hi=dst_hi:
                            V.tensor_tensor(dst_lo, ps, dst_hi, ALU.subtract))

        with (nc.sbuf_tensor(f"wvt_hi_p{s}", [128, EC * E], BF16) as wvt_hi_p_h,
              nc.sbuf_tensor(f"wvt_lo_p{s}", [128, EC * E], BF16) as wvt_lo_p_h,
              nc.sbuf_tensor(f"lnt_hi2{s}", [128, EC * 512], BF16) as lnt_hi2_h,
              nc.sbuf_tensor(f"lnt_lo2{s}", [128, EC * 512], BF16) as lnt_lo2_h):
            wvt_hi_p, wvt_lo_p = wvt_hi_p_h.ap(), wvt_lo_p_h.ap()
            lnt_hi2, lnt_lo2 = lnt_hi2_h.ap(), lnt_lo2_h.ap()
            ser.dma(wvt_hi_p, wvt_hi_d)
            ser.dma(wvt_lo_p, wvt_lo_d)
            for g in range(NCH // 4):
                for tc in range(4):
                    ln1_chunk(s, g * 4 + tc, compute_stats=False)
                    transpose_hilo(lnt_hi2, lnt_lo2, tc)
                for tsub in range(4):
                    c = g * 4 + tsub
                    for h in range(2):
                        ps = ps0 if h == 0 else ps1
                        for f in range(EC):
                            lw_hi = lnt_hi2[:, f * 512 + tsub * 128: f * 512 + (tsub + 1) * 128]
                            lw_lo = lnt_lo2[:, f * 512 + tsub * 128: f * 512 + (tsub + 1) * 128]
                            rh_hi = wvt_hi_p[:, f * E + h * 512: f * E + h * 512 + 512]
                            rh_lo = wvt_lo_p[:, f * E + h * 512: f * E + h * 512 + 512]
                            ser.run(T, T.matmul, ps, lw_hi, rh_hi, start=(f == 0), stop=False)
                            ser.run(T, T.matmul, ps, lw_hi, rh_lo, start=False, stop=False)
                            ser.run(T, T.matmul, ps, lw_lo, rh_hi, start=False,
                                    stop=(f == EC - 1))
                        dst_hi = v_hi[:, c * E + h * 512: c * E + h * 512 + 512]
                        dst_lo = v_lo[:, c * E + h * 512: c * E + h * 512 + 512]
                        ser.run(V, V.tensor_copy, dst_hi, ps)
                        ser.run(V, lambda dst_lo=dst_lo, ps=ps, dst_hi=dst_hi:
                                V.tensor_tensor(dst_lo, ps, dst_hi, ALU.subtract))

        with (nc.sbuf_tensor(f"wt_p{s}", [128, EC * E], BF16) as wt_p_h,
              nc.sbuf_tensor(f"scores{s}", [128, S], F32) as scores_h,
              nc.sbuf_tensor(f"P{s}", [128, S], F32) as P_h,
              nc.sbuf_tensor(f"B{s}", [128, E], F32) as B_h,
              nc.sbuf_tensor(f"C{s}", [128, E], F32) as C_h,
              nc.sbuf_tensor(f"D{s}", [128, E], F32) as D_h,
              nc.sbuf_tensor(f"lnqt_hi{s}", [128, E], BF16) as lnqt_hi_h,
              nc.sbuf_tensor(f"lnqt_lo{s}", [128, E], BF16) as lnqt_lo_h,
              nc.sbuf_tensor(f"qb{s}", [128, E], BF16) as qb_h,
              nc.sbuf_tensor(f"qbt{s}", [128, E], BF16) as qbt_h,
              nc.sbuf_tensor(f"pth{s}", [128, 128], BF16) as pth_h,
              nc.sbuf_tensor(f"ptl{s}", [128, 128], BF16) as ptl_h):
            wt_p, scores, P = wt_p_h.ap(), scores_h.ap(), P_h.ap()
            B, C, D = B_h.ap(), C_h.ap(), D_h.ap()
            lnqt_hi, lnqt_lo = lnqt_hi_h.ap(), lnqt_lo_h.ap()
            qb, qbt = qb_h.ap(), qbt_h.ap()
            pth, ptl = pth_h.ap(), ptl_h.ap()
            ser.dma(wt_p, wt_d)

            for qi in range(NCH):
                W = (qi + 1) * 128
                ln1_chunk(s, qi, compute_stats=False)
                for j in range(EC):
                    ser.run(T, T.transpose, ptb0[:, j * 128:(j + 1) * 128],
                            lnq_hi[:, j * 128:(j + 1) * 128], ident_b)
                    ser.run(T, T.transpose, ptb1[:, j * 128:(j + 1) * 128],
                            lnq_lo[:, j * 128:(j + 1) * 128], ident_b)
                ser.run(V, V.tensor_copy, lnqt_hi, ptb0)
                ser.run(V, V.tensor_copy, lnqt_lo, ptb1)

                ntile = (W + 511) // 512
                for kt in range(ntile):
                    k0 = kt * 512
                    kw = min(512, W - k0)
                    ps = ps0 if kt % 2 == 0 else ps1
                    for j in range(EC):
                        lw_hi = lnqt_hi[:, j * 128:(j + 1) * 128]
                        lw_lo = lnqt_lo[:, j * 128:(j + 1) * 128]
                        rh_hi = kgt_hi[:, j * S + k0: j * S + k0 + kw]
                        rh_lo = kgt_lo[:, j * S + k0: j * S + k0 + kw]
                        ser.run(T, T.matmul, ps[:, 0:kw], lw_hi, rh_hi,
                                start=(j == 0), stop=False)
                        ser.run(T, T.matmul, ps[:, 0:kw], lw_hi, rh_lo,
                                start=False, stop=False)
                        ser.run(T, T.matmul, ps[:, 0:kw], lw_lo, rh_hi,
                                start=False, stop=(j == EC - 1))
                    if kt == ntile - 1:
                        if kw > 128:
                            ser.run(V, V.tensor_copy, scores[:, k0:k0 + kw - 128],
                                    ps[:, 0:kw - 128])
                        ser.run(V, V.tensor_tensor, scores[:, W - 128:W],
                                ps[:, kw - 128:kw], mask, ALU.add)
                    else:
                        ser.run(V, V.tensor_copy, scores[:, k0:k0 + 512], ps)

                ser.run(V, V.reduce_max, col(5), scores[:, 0:W], axis=AX.X, negate=True)
                ser.run(SC, SC.activation, P[:, 0:W], scores[:, 0:W], AF.Exp,
                        bias=col(5), accum_out=col(6))
                ser.run(V, V.reciprocal, col(7), col(6))

                for kb in range(qi + 1):
                    ser.run(T, T.transpose, ptf[:, 0:128],
                            P[:, kb * 128:(kb + 1) * 128], ident_f)
                    ser.run(V, V.tensor_copy, pth, ptf[:, 0:128])
                    ser.run(V, V.tensor_tensor, ptl, ptf[:, 0:128], pth, ALU.subtract)
                    for h in range(2):
                        ps = ps2 if h == 0 else ps3
                        rh_hi = v_hi[:, kb * E + h * 512: kb * E + h * 512 + 512]
                        rh_lo = v_lo[:, kb * E + h * 512: kb * E + h * 512 + 512]
                        ser.run(T, T.matmul, ps, pth, rh_hi, start=(kb == 0), stop=False)
                        ser.run(T, T.matmul, ps, pth, rh_lo, start=False, stop=False)
                        ser.run(T, T.matmul, ps, ptl, rh_hi, start=False,
                                stop=(kb == qi))
                for h in range(2):
                    ps = ps2 if h == 0 else ps3
                    ser.run(V, V.tensor_scalar_mul, A[:, h * 512:(h + 1) * 512], ps,
                            col(7))

                ser.run(SC, SC.activation, B, A, AF.Square, accum_out=col(8))
                ser.run(V, V.reduce_sum, col(9), A, axis=AX.X)
                ser.run(V, V.tensor_scalar_mul, col(9), col(9), 1.0 / E)
                ser.run(V, V.tensor_scalar_mul, col(8), col(8), 1.0 / E)
                ser.run(V, V.tensor_tensor, col(10), col(9), col(9), ALU.mult)
                ser.run(V, V.tensor_tensor, col(8), col(8), col(10), ALU.subtract)
                ser.run(SC, SC.activation, col(10), col(8), AF.Ln, bias=c5)
                ser.run(SC, SC.activation, col(10), col(10), AF.Exp, scale=-0.5)
                ser.run(V, V.tensor_scalar, B, A, col(9), col(10), ALU.subtract, ALU.mult)
                ser.run(V, V.tensor_tensor, C, B, xb, ALU.add)

                ser.run(SC, SC.activation, B, C, AF.Square, accum_out=col(8))
                ser.run(V, V.reduce_sum, col(9), C, axis=AX.X)
                ser.run(V, V.tensor_scalar_mul, col(9), col(9), 1.0 / E)
                ser.run(V, V.tensor_scalar_mul, col(8), col(8), 1.0 / E)
                ser.run(V, V.tensor_tensor, col(10), col(9), col(9), ALU.mult)
                ser.run(V, V.tensor_tensor, col(8), col(8), col(10), ALU.subtract)
                ser.run(SC, SC.activation, col(10), col(8), AF.Ln, bias=c5)
                ser.run(SC, SC.activation, col(10), col(10), AF.Exp, scale=-0.5)
                ser.run(V, V.tensor_scalar_add, col(11), col(8), 1e-5)
                ser.run(V, V.reciprocal, col(11), col(11))
                ser.run(V, V.tensor_tensor, col(11), col(8), col(11), ALU.mult)
                ser.run(SC, SC.activation, col(11), col(11), AF.Ln, bias=c6)
                ser.run(SC, SC.activation, col(11), col(11), AF.Exp, scale=-0.5)
                ser.run(V, V.tensor_tensor, col(11), col(10), col(11), ALU.mult)
                ser.run(V, V.tensor_scalar_sub, B, C, col(9))
                ser.run(V, V.reduce_max, col(12), B, axis=AX.X, apply_absolute_value=True)
                ser.run(V, V.tensor_tensor, col(12), col(12), col(11), ALU.mult)
                ser.run(V, V.tensor_scalar_max, col(12), col(12), 1e-5)
                ser.run(V, V.reciprocal, col(13), col(12))
                ser.run(V, V.tensor_scalar_mul, col(13), col(13), 127.0)
                ser.run(V, V.tensor_tensor, col(14), col(11), col(13), ALU.mult)
                ser.run(V, V.tensor_scalar, A, B, col(14), 127.5, ALU.mult, ALU.add)
                for lev, cbit in enumerate([128.0, 64.0, 32.0, 16.0, 8.0, 4.0, 2.0, 1.0]):
                    if lev == 0:
                        w_src, w_dst = A, D
                    else:
                        w_src, w_dst = (D, B) if lev % 2 == 1 else (B, D)
                    ser.run(V, V.tensor_single_scalar, lnq_hi, w_src, cbit, ALU.is_ge)
                    ser.run(V, lambda w_dst=w_dst, w_src=w_src, cbit=cbit:
                            V.scalar_tensor_tensor(w_dst, lnq_hi, -cbit, w_src,
                                                   ALU.mult, ALU.add))
                ser.run(V, V.tensor_tensor, lnq_lo, A, B, ALU.subtract)
                ser.run(V, V.tensor_single_scalar, qb, lnq_lo, 127.0, ALU.subtract)
                ser.run(V, V.reciprocal, col(15), col(13))
                ser.run(V, V.tensor_tensor, col(15), col(15), ws_bc, ALU.mult)
                for j in range(EC):
                    ser.run(T, T.transpose, ptb0[:, j * 128:(j + 1) * 128],
                            qb[:, j * 128:(j + 1) * 128], ident_b)
                ser.run(V, V.tensor_copy, qbt, ptb0)
                for h in range(2):
                    ps = ps2 if h == 0 else ps3
                    for j in range(EC):
                        ser.run(T, T.matmul, ps, qbt[:, j * 128:(j + 1) * 128],
                                wt_p[:, j * E + h * 512: j * E + h * 512 + 512],
                                start=(j == 0), stop=(j == EC - 1))
                    ser.run(V, lambda h=h, ps=ps: V.scalar_tensor_tensor(
                        B[:, h * 512:(h + 1) * 512], ps, col(15),
                        bias_bc[:, h * 512:(h + 1) * 512], ALU.mult, ALU.add))

                ser.run(SC, SC.activation, A, B, AF.Square, accum_out=col(8))
                ser.run(V, V.reduce_sum, col(9), B, axis=AX.X)
                ser.run(V, V.tensor_scalar_mul, col(9), col(9), 1.0 / E)
                ser.run(V, V.tensor_scalar_mul, col(8), col(8), 1.0 / E)
                ser.run(V, V.tensor_tensor, col(10), col(9), col(9), ALU.mult)
                ser.run(V, V.tensor_tensor, col(8), col(8), col(10), ALU.subtract)
                ser.run(SC, SC.activation, col(10), col(8), AF.Ln, bias=c5)
                ser.run(SC, SC.activation, col(10), col(10), AF.Exp, scale=-0.5)
                ser.run(V, V.tensor_scalar, A, B, col(9), col(10), ALU.subtract, ALU.mult)
                ser.run(V, V.tensor_tensor, B, A, C, ALU.add)
                ser.dma(out_d[s, qi * 128:(qi + 1) * 128, :], B)

    nc.gpsimd.wait_ge(ser.ch, ser.cnt)
    return nc


_NC_CACHE = {}


def _get_nc(NSEQ, S):
    key = (NSEQ, S)
    if key not in _NC_CACHE:
        _NC_CACHE[key] = build_nc(NSEQ, S)
    return _NC_CACHE[key]


def kernel(L0_attn_x, L0_attn_WQ, L0_attn_WK, L0_attn_WV,
           L0_ffn_w_int8, L0_ffn_weight_scale, L0_ffn_bias):
    x = np.ascontiguousarray(np.asarray(L0_attn_x, dtype=np.float32))
    wq = np.ascontiguousarray(np.asarray(L0_attn_WQ, dtype=np.float32))
    wk = np.ascontiguousarray(np.asarray(L0_attn_WK, dtype=np.float32))
    wv = np.ascontiguousarray(np.asarray(L0_attn_WV, dtype=np.float32))
    wf = np.ascontiguousarray(np.asarray(L0_ffn_w_int8, dtype=np.int32))
    ws = np.asarray(L0_ffn_weight_scale, dtype=np.float32).reshape(1, 1)
    bias = np.asarray(L0_ffn_bias, dtype=np.float32).reshape(1, E)

    Bt, S = x.shape[0], x.shape[1]
    NSEQ = Bt // NCORE
    nc = _get_nc(NSEQ, S)
    in_maps = []
    for c in range(NCORE):
        in_maps.append({
            "x": x[c * NSEQ:(c + 1) * NSEQ],
            "wq": wq, "wk": wk, "wv": wv, "wf": wf,
            "wsc": ws, "bias": bias,
        })
    res = run_bass_kernel_spmd(nc, in_maps, core_ids=list(range(NCORE)))
    out = np.concatenate([r["out"] for r in res.results], axis=0)
    return out.astype(np.float32, copy=False)


# version 16
# speedup vs baseline: 1.8491x; 1.8491x over previous
import contextlib
import numpy as np
import concourse.bass as bass
import concourse.mybir as mybir
from concourse.bass_utils import run_bass_kernel_spmd

F32 = mybir.dt.float32
BF16 = mybir.dt.bfloat16
I32 = mybir.dt.int32
AF = mybir.ActivationFunctionType
ALU = mybir.AluOpType
AX = mybir.AxisListType

E = 1024
EC = 8
NCORE = 8
GW = 256


class Tok:
    __slots__ = ("sem", "val")

    def __init__(self, sem, val):
        self.sem = sem
        self.val = val


class EngS:

    def __init__(self, nc, eng, name, serial=False):
        self.eng = eng
        self.sem = nc.alloc_semaphore(name)
        self.cnt = 0
        self.seen = {}
        self.serial = serial

    def wait(self, deps):
        for t in deps:
            if t is None or t.sem is self.sem:
                continue
            k = id(t.sem)
            if self.seen.get(k, 0) >= t.val:
                continue
            self.eng.wait_ge(t.sem, t.val)
            self.seen[k] = t.val

    def run(self, fn, *args, deps=(), tok=True, **kw):
        if self.serial and self.cnt > 0:
            self.eng.wait_ge(self.sem, self.cnt)
        self.wait(deps)
        inst = fn(*args, **kw)
        if tok or self.serial:
            self.cnt += 1
            inst.then_inc(self.sem, 1)
            return Tok(self.sem, self.cnt)
        return None

    def tok(self):
        return Tok(self.sem, self.cnt)


class DQ:

    def __init__(self, nc, eng, name):
        self.eng = eng
        self.sem = nc.alloc_semaphore(name)
        self.cnt = 0
        self.seen = {}

    def wait(self, deps):
        for t in deps:
            if t is None or t.sem is self.sem:
                continue
            k = id(t.sem)
            if self.seen.get(k, 0) >= t.val:
                continue
            self.eng.wait_ge(t.sem, t.val)
            self.seen[k] = t.val

    def dma(self, dst, src, deps=()):
        for t in deps:
            if t is None or t.sem is self.sem:
                continue
            k = id(t.sem)
            if self.seen.get(k, 0) >= t.val:
                continue
            self.eng.wait_ge(t.sem, t.val)
            self.seen[k] = t.val
        self.eng.dma_start(out=dst, in_=src).then_inc(self.sem, 16)
        self.cnt += 16
        return Tok(self.sem, self.cnt)

    def full(self):
        return Tok(self.sem, self.cnt)


def build_nc(NSEQ, S):
    NCH = S // 128
    NG = S // GW
    nc = bass.Bass("TRN2", target_bir_lowering=False)

    x_d = nc.dram_tensor("x", [NSEQ, S, E], F32, kind="ExternalInput").ap()
    wq_d = nc.dram_tensor("wq", [E, E], F32, kind="ExternalInput").ap()
    wk_d = nc.dram_tensor("wk", [E, E], F32, kind="ExternalInput").ap()
    wv_d = nc.dram_tensor("wv", [E, E], F32, kind="ExternalInput").ap()
    wf_d = nc.dram_tensor("wf", [E, E], I32, kind="ExternalInput").ap()
    ws_d = nc.dram_tensor("wsc", [1, 1], F32, kind="ExternalInput").ap()
    bias_d = nc.dram_tensor("bias", [1, E], F32, kind="ExternalInput").ap()
    out_d = nc.dram_tensor("out", [NSEQ, S, E], F32, kind="ExternalOutput").ap()

    ht_hi_d = nc.dram_tensor("ht_hi_s", [128, EC * E], BF16, kind="Internal").ap()
    ht_lo_d = nc.dram_tensor("ht_lo_s", [128, EC * E], BF16, kind="Internal").ap()
    wvt_hi_d = nc.dram_tensor("wvt_hi_s", [128, EC * E], BF16, kind="Internal").ap()
    wvt_lo_d = nc.dram_tensor("wvt_lo_s", [128, EC * E], BF16, kind="Internal").ap()
    wt_d = nc.dram_tensor("wt_s", [128, EC * E], BF16, kind="Internal").ap()
    lnt_d = nc.dram_tensor("lnt_s", [NSEQ * NG, 128, 2 * 2048], BF16,
                           kind="Internal").ap()

    PE = EngS(nc, nc.tensor, "pe")
    DV = EngS(nc, nc.vector, "dv", serial=True)
    AC = EngS(nc, nc.scalar, "ac", serial=True)
    PL = EngS(nc, nc.gpsimd, "pl", serial=True)
    QS = DQ(nc, nc.sync, "qs")
    QW = DQ(nc, nc.sync, "qw")
    QSP = DQ(nc, nc.sync, "qsp")
    QX0 = DQ(nc, nc.sync, "qx0")
    QX1 = DQ(nc, nc.sync, "qx1")
    QL0 = DQ(nc, nc.sync, "ql0")
    QL1 = DQ(nc, nc.sync, "ql1")
    QO = DQ(nc, nc.sync, "qo")
    ALLQ = (QS, QW, QSP, QX0, QX1, QL0, QL1, QO)

    T, V, SC, G = nc.tensor, nc.vector, nc.scalar, nc.gpsimd

    def barrier():
        toks = [PE.tok(), DV.tok(), AC.tok(), PL.tok()] + \
            [Tok(q.sem, q.cnt) for q in ALLQ]
        for e in (PE, DV, AC, PL) + tuple(ALLQ):
            e.wait(toks)

    def agg_explicit(bns, mcol, vcol, base, deps=()):
        p4 = st[:, base:base + 4]
        sm = st[:, base + 4:base + 5]
        scv = st[:, base + 5:base + 6]
        s2 = st[:, base + 6:base + 7]
        ms = bns[:, 1:12:3]
        cvs = bns[:, 2:12:3]
        DV.run(V.tensor_reduce, sm, ms, axis=AX.X, op=ALU.add, deps=deps,
               tok=False)
        DV.run(V.tensor_reduce, scv, cvs, axis=AX.X, op=ALU.add, tok=False)
        DV.run(V.tensor_tensor, p4, ms, ms, ALU.mult, tok=False)
        DV.run(V.tensor_reduce, s2, p4, axis=AX.X, op=ALU.add, tok=False)
        DV.run(V.tensor_scalar_mul, mcol, sm, 0.25, tok=False)
        DV.run(V.tensor_scalar_mul, scv, scv, 1.0 / 1024.0, tok=False)
        DV.run(V.scalar_tensor_tensor, s2, s2, 0.25, scv, ALU.mult, ALU.add,
               tok=False)
        DV.run(V.tensor_tensor, sm, mcol, mcol, ALU.mult, tok=False)
        return DV.run(V.tensor_tensor, vcol, s2, sm, ALU.subtract)

    sb = lambda nm, shp, dt=F32: nc.alloc_sbuf_tensor(nm, shp, dt)

    PS = [nc.alloc_psum_tensor(f"ps{i}", [128, 512], F32).ap() for i in range(6)]
    PTB = nc.alloc_psum_tensor("ptb", [128, 1024], BF16).ap()
    PSD = nc.alloc_psum_tensor("psd", [128, 16], F32).ap()

    ident_f = sb("ident_f", [128, 128]).ap()
    ident_b = sb("ident_b", [128, 128], BF16).ap()
    maskA = sb("maskA", [128, GW], BF16).ap()
    ones8 = sb("ones8", [128, 8], BF16).ap()
    ws_bc = sb("ws_bc", [128, 1]).ap()
    bias_bc = sb("bias_bc", [128, E]).ap()
    c5 = sb("c5", [128, 1]).ap()
    st = sb("stc", [128, 64]).ap()

    def col(i):
        return st[:, i:i + 1]

    PL.run(G.memset, c5, 1e-5)
    PL.run(G.memset, ident_f, 1.0)
    t_idf = PL.run(G.affine_select, out=ident_f, in_=ident_f,
                   compare_op=ALU.is_equal, fill=0.0, base=0,
                   pattern=[[-1, 128]], channel_multiplier=1)
    t_idb = DV.run(V.tensor_copy, ident_b, ident_f, deps=[t_idf])
    PL.run(G.memset, maskA, 1.0)
    t_mask = PL.run(G.affine_select, out=maskA, in_=maskA,
                    compare_op=ALU.is_ge, fill=0.0, base=0,
                    pattern=[[1, GW]], channel_multiplier=-1)
    PL.run(G.memset, ones8, 0.0)
    t_ones8 = PL.run(G.memset, ones8[:, 0:1], 1.0)

    t_ws = QW.dma(ws_bc, ws_d.partition_broadcast(128))
    t_wsbc = DV.run(V.tensor_scalar_mul, ws_bc, ws_bc, -1.0 / 127.0,
                    deps=[t_ws])
    t_bb1 = QW.dma(bias_bc, bias_d.partition_broadcast(128),
                   deps=[t_ws])
    t_bb0 = t_bb1

    with contextlib.ExitStack() as sst:
        stg_a = sst.enter_context(nc.sbuf_tensor("stg_a", [128, E], F32)).ap()
        stg_b = sst.enter_context(nc.sbuf_tensor("stg_b", [128, E], F32)).ap()
        stg_h = sst.enter_context(nc.sbuf_tensor("stg_h", [128, E], BF16)).ap()
        stg_l = sst.enter_context(nc.sbuf_tensor("stg_l", [128, E], BF16)).ap()
        bigp = sst.enter_context(nc.sbuf_tensor("big", [128, EC * E], BF16)).ap()
        stgi = sst.enter_context(nc.sbuf_tensor("stg_i", [128, E], I32)).ap()

        big_r = bigp.rearrange("p (j c) -> p j c", j=EC)
        ptb_r = PTB.rearrange("p (j c) -> p j c", j=EC)

        t_ev = DV.tok()
        for i in range(EC):
            t_d = QW.dma(stgi, wf_d[i * 128:(i + 1) * 128, :], deps=[t_ev])
            t_c = DV.run(V.tensor_copy, stg_h, stgi, deps=[t_d])
            t_tr = None
            for j in range(EC):
                t_tr = PE.run(T.transpose, PTB[:, j * 128:(j + 1) * 128],
                              stg_h[:, j * 128:(j + 1) * 128], ident_b,
                              deps=[t_c, t_idb, t_ev] if j == 0 else (),
                              tok=(j == EC - 1))
            t_ev = DV.run(V.tensor_copy, big_r[:, :, i * 128:(i + 1) * 128],
                          ptb_r, deps=[t_tr])
        t_wt_sp = QSP.dma(wt_d, bigp, deps=[t_ev])

        t_wv_sp = t_wt_sp
        for half, dst_d in ((0, wvt_hi_d), (1, wvt_lo_d)):
            t_ev = DV.tok()
            for i in range(EC):
                t_d = QW.dma(stg_a, wv_d[i * 128:(i + 1) * 128, :], deps=[t_ev])
                t_h = AC.run(SC.copy, stg_h, stg_a, deps=[t_d])
                if half == 0:
                    src, t_src = stg_h, t_h
                else:
                    t_src = DV.run(V.tensor_tensor, stg_l, stg_a, stg_h,
                                   ALU.subtract, deps=[t_h])
                    src = stg_l
                t_tr = None
                for j in range(EC):
                    t_tr = PE.run(T.transpose, PTB[:, j * 128:(j + 1) * 128],
                                  src[:, j * 128:(j + 1) * 128], ident_b,
                                  deps=[t_src, t_ev] if j == 0 else (),
                                  tok=(j == EC - 1))
                t_ev = DV.run(V.tensor_copy, big_r[:, :, i * 128:(i + 1) * 128],
                              ptb_r, deps=[t_tr, t_wv_sp if i == 0 else None])
            t_wv_sp = QSP.dma(dst_d, bigp, deps=[t_ev])
        t_wvl_sp = t_wv_sp

        wqh = sst.enter_context(nc.sbuf_tensor("wqs_h", [128, EC * E], BF16)).ap()
        wql = sst.enter_context(nc.sbuf_tensor("wqs_l", [128, EC * E], BF16)).ap()
        wkh = sst.enter_context(nc.sbuf_tensor("wks_h", [128, EC * E], BF16)).ap()
        wkl = sst.enter_context(nc.sbuf_tensor("wks_l", [128, EC * E], BF16)).ap()
        t_tt = None
        for d in range(EC):
            for w_d, dh, dl, stg in ((wq_d, wqh, wql, stg_a),
                                     (wk_d, wkh, wkl, stg_b)):
                t_d = QW.dma(stg, w_d[d * 128:(d + 1) * 128, :], deps=[t_tt])
                t_h = AC.run(SC.copy, dh[:, d * E:(d + 1) * E], stg, deps=[t_d])
                t_tt = DV.run(V.tensor_tensor, dl[:, d * E:(d + 1) * E], stg,
                              dh[:, d * E:(d + 1) * E], ALU.subtract, deps=[t_h])
        t_split = t_tt
        t_ev = [None, None]
        t_lo_sp = None
        for f in range(EC):
            for h in range(2):
                ps = PS[h]
                t_mm = None
                for d in range(EC):
                    lw_h = wkh[:, d * E + f * 128: d * E + (f + 1) * 128]
                    lw_l = wkl[:, d * E + f * 128: d * E + (f + 1) * 128]
                    rh_h = wqh[:, d * E + h * 512: d * E + h * 512 + 512]
                    rh_l = wql[:, d * E + h * 512: d * E + h * 512 + 512]
                    dep0 = [t_split, t_ev[h]] if d == 0 else ()
                    PE.run(T.matmul, ps, lw_h, rh_h, start=(d == 0), stop=False,
                           deps=dep0, tok=False)
                    PE.run(T.matmul, ps, lw_h, rh_l, start=False, stop=False,
                           tok=False)
                    t_mm = PE.run(T.matmul, ps, lw_l, rh_h, start=False,
                                  stop=(d == EC - 1), tok=(d == EC - 1))
                dst_h = bigp[:, f * E + h * 512: f * E + h * 512 + 512]
                t_hh = AC.run(SC.activation, dst_h, ps, AF.Identity, scale=0.03125,
                              deps=[t_mm, t_wvl_sp if f == 0 and h == 0 else None])
                t_ev[h] = DV.run(V.scalar_tensor_tensor,
                                 stg_l[:, h * 512:(h + 1) * 512], ps, 0.03125,
                                 dst_h, ALU.mult, ALU.subtract,
                                 deps=[t_hh, t_lo_sp])
            t_lo_sp = QSP.dma(ht_lo_d[:, f * E:(f + 1) * E], stg_l[:, 0:E],
                              deps=[t_ev[0], t_ev[1]])
        t_ht_hi_sp = QSP.dma(ht_hi_d, bigp, deps=[t_ev[0], t_ev[1]])
        t_ht_lo_sp = t_lo_sp

    for s in range(NSEQ):
        with contextlib.ExitStack() as seq_stack:
            kgh = seq_stack.enter_context(
                nc.sbuf_tensor(f"kgt_h{s}", [128, EC * S], BF16)).ap()
            kgl = seq_stack.enter_context(
                nc.sbuf_tensor(f"kgt_l{s}", [128, EC * S], BF16)).ap()

            with contextlib.ExitStack() as pa:
                hth = pa.enter_context(
                    nc.sbuf_tensor(f"ht_h{s}", [128, EC * E], BF16)).ap()
                htl = pa.enter_context(
                    nc.sbuf_tensor(f"ht_l{s}", [128, EC * E], BF16)).ap()
                xa = pa.enter_context(
                    nc.sbuf_tensor(f"xa{s}", [128, 2 * E], F32)).ap()
                aa = pa.enter_context(
                    nc.sbuf_tensor(f"aa{s}", [128, 2 * E], F32)).ap()
                lhtm = pa.enter_context(
                    nc.sbuf_tensor(f"lhtm{s}", [128, 2 * E], BF16)).ap()
                lltm = pa.enter_context(
                    nc.sbuf_tensor(f"lltm{s}", [128, 2 * E], BF16)).ap()
                lh = pa.enter_context(
                    nc.sbuf_tensor(f"lh{s}", [128, 2 * E], BF16)).ap()
                ll = pa.enter_context(
                    nc.sbuf_tensor(f"ll{s}", [128, 2 * E], BF16)).ap()
                bn = pa.enter_context(
                    nc.sbuf_tensor(f"bn{s}", [128, 2 * 12], F32)).ap()

                barrier()
                t_a_dv = DV.tok()
                t_a_ptb = DV.tok()
                t_hth = QW.dma(hth, ht_hi_d, deps=[QSP.full()])
                t_htl = QW.dma(htl, ht_lo_d)
                t_x = [None, None]
                t_ap = [None, None]
                t_rs = [None, None]
                t_sp_tm = [None, None]
                t_kg = [t_a_dv, t_a_dv]
                t_spill = None
                t_kgpe = None
                rh_h = lh.rearrange("p (c j t) -> p c j t", c=2, j=EC)
                rh_l = ll.rearrange("p (c j t) -> p c j t", c=2, j=EC)
                for g in range(NG):
                    for ci in range(2):
                        c = 2 * g + ci
                        sl = ci
                        xs = xa[:, sl * E:(sl + 1) * E]
                        As = aa[:, sl * E:(sl + 1) * E]
                        hs = lhtm[:, sl * E:(sl + 1) * E]
                        ls = lltm[:, sl * E:(sl + 1) * E]
                        bns = bn[:, sl * 12:(sl + 1) * 12]
                        t_x[sl] = (QX0 if sl == 0 else QX1).dma(
                            xs, x_d[s, c * 128:(c + 1) * 128, :],
                            deps=[t_ap[sl]])
                        t_sq = AC.run(SC.activation, As, xs, AF.Square,
                                      accum_out=col(48 + 8 * sl),
                                      deps=[t_x[sl], t_sp_tm[sl], t_ap[sl]])
                        DV.run(V.reduce_sum, col(24 + 4 * sl), xs, axis=AX.X,
                               deps=[t_x[sl], t_rs[sl]], tok=False)
                        DV.run(V.tensor_scalar_mul, st[:, 2 * sl:2 * sl + 1],
                               col(24 + 4 * sl), 1.0 / 1024.0, tok=False)
                        DV.run(V.tensor_scalar_mul, col(25 + 4 * sl),
                               col(48 + 8 * sl), 1.0 / 1024.0, deps=[t_sq],
                               tok=False)
                        DV.run(V.tensor_tensor, col(26 + 4 * sl),
                               st[:, 2 * sl:2 * sl + 1],
                               st[:, 2 * sl:2 * sl + 1], ALU.mult, tok=False)
                        t_ag = DV.run(V.tensor_tensor,
                                      st[:, 2 * sl + 1:2 * sl + 2],
                                      col(25 + 4 * sl), col(26 + 4 * sl),
                                      ALU.subtract)
                        AC.run(SC.activation, col(32 + 8 * sl),
                               st[:, 2 * sl + 1:2 * sl + 2], AF.Ln, bias=c5,
                               deps=[t_ag], tok=False)
                        t_r = AC.run(SC.activation, col(32 + 8 * sl), col(32 + 8 * sl),
                                     AF.Exp, scale=-0.5)
                        t_rs[sl] = t_r
                        t_ap[sl] = DV.run(V.tensor_scalar, As, xs,
                                          st[:, 2 * sl:2 * sl + 1], col(32 + 8 * sl),
                                          ALU.subtract, ALU.mult,
                                          deps=[t_r, t_sq, t_sp_tm[sl]])
                        t_hi = AC.run(SC.copy, hs, As,
                                      deps=[t_ap[sl], t_spill, t_kgpe])
                        t_lo = DV.run(V.tensor_tensor, ls, As, hs, ALU.subtract,
                                      deps=[t_hi, t_spill, t_kgpe])
                        t_sp_tm[sl] = t_lo
                        lhs = lh[:, ci * E:(ci + 1) * E]
                        lls = ll[:, ci * E:(ci + 1) * E]
                        for half, srct, dstt, t_dep in ((0, hs, lhs, t_hi),
                                                        (1, ls, lls, t_lo)):
                            t_tr = None
                            for j in range(EC):
                                t_tr = PE.run(
                                    T.transpose, PTB[:, j * 128:(j + 1) * 128],
                                    srct[:, j * 128:(j + 1) * 128], ident_b,
                                    deps=[t_dep, t_a_ptb] if j == 0 else (),
                                    tok=(j == EC - 1))
                            t_a_ptb = DV.run(V.tensor_copy, dstt, PTB,
                                             deps=[t_tr, t_spill])
                    t_lh = t_a_ptb
                    t_ll = t_a_ptb
                    for e in range(EC):
                        ps = (PS[4] if e % 2 == 0 else PS[5])[:, 0:256]
                        t_mm = None
                        for f in range(EC):
                            lw_h = hth[:, f * E + e * 128: f * E + (e + 1) * 128]
                            lw_l = htl[:, f * E + e * 128: f * E + (e + 1) * 128]
                            dep0 = [t_lh, t_ll, t_kg[e % 2], t_htl] \
                                if f == 0 else ()
                            PE.run(T.matmul, ps, lw_h, rh_h[:, :, f, :],
                                   start=(f == 0), stop=False, deps=dep0,
                                   tok=False)
                            PE.run(T.matmul, ps, lw_h, rh_l[:, :, f, :],
                                   start=False, stop=False, tok=False)
                            t_mm = PE.run(T.matmul, ps, lw_l, rh_h[:, :, f, :],
                                          start=False, stop=(f == EC - 1),
                                          tok=(f == EC - 1))
                        dst_h = kgh[:, e * S + g * GW: e * S + (g + 1) * GW]
                        dst_l = kgl[:, e * S + g * GW: e * S + (g + 1) * GW]
                        t_eh = AC.run(SC.copy, dst_h, ps, deps=[t_mm])
                        t_kg[e % 2] = DV.run(V.tensor_tensor, dst_l, ps, dst_h,
                                             ALU.subtract, deps=[t_eh])
                    t_kgpe = PE.tok()
                    QSP.dma(lnt_d[s * NG + g, :, 0:2048], lh,
                            deps=[t_lh, t_ll])
                    t_spill = QSP.dma(lnt_d[s * NG + g, :, 2048:4096], ll)
                t_kg_done = [t_kg[0], t_kg[1]]

            with contextlib.ExitStack() as pb:
                vh = pb.enter_context(
                    nc.sbuf_tensor(f"v_h{s}", [128, NCH * E], BF16)).ap()
                vl = pb.enter_context(
                    nc.sbuf_tensor(f"v_l{s}", [128, NCH * E], BF16)).ap()

                with contextlib.ExitStack() as pbb:
                    wvh = pbb.enter_context(
                        nc.sbuf_tensor(f"wvh{s}", [128, EC * E], BF16)).ap()
                    wvl = pbb.enter_context(
                        nc.sbuf_tensor(f"wvl{s}", [128, EC * E], BF16)).ap()
                    lqh = pbb.enter_context(
                        nc.sbuf_tensor(f"lqh{s}", [128, 2, 2 * E], BF16)).ap()
                    lql = pbb.enter_context(
                        nc.sbuf_tensor(f"lql{s}", [128, 2, 2 * E], BF16)).ap()

                    barrier()
                    t_b_dv = DV.tok()
                    t_b_ac = AC.tok()
                    t_wvh = QW.dma(wvh, wvt_hi_d, deps=[QSP.full()])
                    t_wvl = QW.dma(wvl, wvt_lo_d)
                    t_lq = [None, None]
                    t_vev = [t_b_dv, t_b_dv]
                    for g in range(NG):
                        gb = g % 2
                        _ql = QL0 if gb == 0 else QL1
                        _ql.dma(lqh[:, gb, :], lnt_d[s * NG + g, :, 0:2048],
                                deps=[t_lq[gb], QSP.full()])
                        t_lq1 = _ql.dma(lql[:, gb, :],
                                        lnt_d[s * NG + g, :, 2048:4096])
                        for ci in range(2):
                            c = 2 * g + ci
                            ps_h = PS[2 * (c % 2)]
                            ps_l = PS[2 * (c % 2) + 1]
                            t_mm = None
                            for j in range(EC):
                                lw_h = lqh[:, gb,
                                           ci * E + j * 128: ci * E + (j + 1) * 128]
                                lw_l = lql[:, gb,
                                           ci * E + j * 128: ci * E + (j + 1) * 128]
                                dep0 = [t_lq1, t_vev[c % 2], t_wvl, t_b_ac] \
                                    if j == 0 else ()
                                for h, pps in ((0, ps_h), (1, ps_l)):
                                    rr_h = wvh[:, j * E + h * 512:
                                               j * E + h * 512 + 512]
                                    rr_l = wvl[:, j * E + h * 512:
                                               j * E + h * 512 + 512]
                                    PE.run(T.matmul, pps, lw_h, rr_h,
                                           start=(j == 0), stop=False,
                                           deps=dep0 if h == 0 else (), tok=False)
                                    PE.run(T.matmul, pps, lw_h, rr_l, start=False,
                                           stop=False, tok=False)
                                    t_mm = PE.run(T.matmul, pps, lw_l, rr_h,
                                                  start=False, stop=(j == EC - 1),
                                                  tok=(j == EC - 1 and h == 1))
                            dst_h = vh[:, c * E:(c + 1) * E]
                            dst_l = vl[:, c * E:(c + 1) * E]
                            t_e0 = AC.run(SC.copy, dst_h[:, 0:512], ps_h,
                                          deps=[t_mm])
                            t_e1 = AC.run(SC.copy, dst_h[:, 512:1024], ps_l)
                            DV.run(V.tensor_tensor, dst_l[:, 0:512], ps_h,
                                   dst_h[:, 0:512], ALU.subtract, deps=[t_e0],
                                   tok=False)
                            t_vev[c % 2] = DV.run(
                                V.tensor_tensor, dst_l[:, 512:1024], ps_l,
                                dst_h[:, 512:1024], ALU.subtract, deps=[t_e1])
                        t_lq[gb] = PE.tok()
                    t_v_done = [t_vev[0], t_vev[1]]
                    t_v_ac = AC.tok()

                with contextlib.ExitStack() as pq:
                    wt = pq.enter_context(
                        nc.sbuf_tensor(f"wt{s}", [128, EC * E], BF16)).ap()
                    lqh = pq.enter_context(
                        nc.sbuf_tensor(f"lqh2{s}", [128, 2 * E], BF16)).ap()
                    lql = pq.enter_context(
                        nc.sbuf_tensor(f"lql2{s}", [128, 2 * E], BF16)).ap()
                    pth = pq.enter_context(
                        nc.sbuf_tensor(f"pth{s}", [128, 4 * GW], BF16)).ap()
                    ptl = pq.enter_context(
                        nc.sbuf_tensor(f"ptl{s}", [128, 4 * GW], BF16)).ap()
                    scb = pq.enter_context(
                        nc.sbuf_tensor(f"sct{s}", [128, 2 * GW], F32)).ap()
                    agt = pq.enter_context(
                        nc.sbuf_tensor(f"agt{s}", [128, 2 * E], F32)).ap()
                    xct = pq.enter_context(
                        nc.sbuf_tensor(f"xct{s}", [128, 2 * E], F32)).ap()
                    tgt = pq.enter_context(
                        nc.sbuf_tensor(f"tgt{s}", [128, 2 * E], F32)).ap()
                    uqt = pq.enter_context(
                        nc.sbuf_tensor(f"uqt{s}", [128, 2 * E], F32)).ap()
                    indb = pq.enter_context(
                        nc.sbuf_tensor(f"indt{s}", [128, 2 * E], BF16)).ap()
                    qbt = pq.enter_context(
                        nc.sbuf_tensor(f"qbtt{s}", [128, E], BF16)).ap()
                    bnq = pq.enter_context(
                        nc.sbuf_tensor(f"bnq{s}", [128, 24], F32)).ap()

                    barrier()
                    t_q_dv = DV.tok()
                    t_q_ac = AC.tok()
                    t_wt = QW.dma(wt, wt_d, deps=[QSP.full()])
                    t_lq = None
                    t_lqd = None
                    t_xd = None
                    t_exp_half = [t_q_ac, t_q_ac]
                    t_scs_half = [t_q_dv, t_q_dv]
                    t_pt_slot = [t_q_dv] * 4
                    t_av_slot = [None] * 4
                    t_avev = [t_v_ac, t_v_ac]
                    t_ffne = [t_v_done[1], t_v_done[1]]
                    t_ffnp = [None, None]
                    t_den = [None, None]
                    t_qb = [t_q_dv, t_q_dv]
                    t_g4tok = [None, None]
                    t_epi = [None, None]
                    t_out = [None, None]

                    QL0.dma(lqh, lnt_d[s * NG + 0, :, 0:2048],
                            deps=[QSP.full()])
                    t_lqd = QL0.dma(lql, lnt_d[s * NG + 0, :, 2048:4096])
                    t_xd = QX0.dma(
                        xct.rearrange("p (c e) -> p c e", c=2),
                        x_d[s, 0:256, :].rearrange("(c p) e -> p c e", c=2))

                    for g in range(NG):
                        kmax = 2 * g + 2
                        r4h = lqh.rearrange("p (c j t) -> p c j t", c=2, j=EC)
                        r4l = lql.rearrange("p (c j t) -> p c j t", c=2, j=EC)
                        t_st = {}
                        t_pt = {}

                        def st_tile(kb):
                            odd_diag = (kb == 2 * g + 1)
                            w = 128 if odd_diag else 256
                            ps = (PS[0] if kb % 2 == 0 else PS[1])[:, 0:w]
                            t_mm = None
                            for j in range(EC):
                                lw_h = kgh[:, j * S + kb * 128:
                                           j * S + (kb + 1) * 128]
                                lw_l = kgl[:, j * S + kb * 128:
                                           j * S + (kb + 1) * 128]
                                rj_h = r4h[:, 1, j, :] if odd_diag \
                                    else r4h[:, :, j, :]
                                rj_l = r4l[:, 1, j, :] if odd_diag \
                                    else r4l[:, :, j, :]
                                dep0 = []
                                if j == 0:
                                    dep0 = [t_lqd, t_exp_half[kb % 2],
                                            t_v_done[1]]
                                PE.run(T.matmul, ps, lw_h, rj_h, start=(j == 0),
                                       stop=False, deps=dep0, tok=False)
                                PE.run(T.matmul, ps, lw_h, rj_l, start=False,
                                       stop=False, tok=False)
                                t_mm = PE.run(T.matmul, ps, lw_l, rj_h,
                                              start=False, stop=(j == EC - 1),
                                              tok=(j == EC - 1))
                            t_st[kb] = t_mm

                        def exp_split(kb):
                            odd_diag = (kb == 2 * g + 1)
                            w = 128 if odd_diag else 256
                            ps = (PS[0] if kb % 2 == 0 else PS[1])[:, 0:w]
                            scs = scb[:, (kb % 2) * GW: (kb % 2) * GW + w]
                            t_e = AC.run(SC.activation, scs, ps, AF.Exp,
                                         deps=[t_st[kb], t_scs_half[kb % 2]])
                            t_exp_half[kb % 2] = t_e
                            psl = kb % 4
                            ph = pth[:, psl * GW: psl * GW + w]
                            pl_ = ptl[:, psl * GW: psl * GW + w]
                            DV.run(V.tensor_copy, ph, scs,
                                   deps=[t_e, t_av_slot[psl]], tok=False)
                            t_l = DV.run(V.tensor_tensor, pl_, scs, ph,
                                         ALU.subtract)
                            t_scs_half[kb % 2] = t_l
                            if kb == 2 * g:
                                DV.run(V.tensor_tensor, ph, ph, maskA, ALU.mult,
                                       deps=[t_mask], tok=False)
                                t_l = DV.run(V.tensor_tensor, pl_, pl_, maskA,
                                             ALU.mult)
                            elif odd_diag:
                                DV.run(V.tensor_tensor, ph, ph, maskA[:, 0:128],
                                       ALU.mult, deps=[t_mask], tok=False)
                                t_l = DV.run(V.tensor_tensor, pl_, pl_,
                                             maskA[:, 0:128], ALU.mult)
                            t_pt[kb] = t_l

                        t_avdone = [None, None]

                        def av_tile(kb):
                            psl = kb % 4
                            odd_diag = (kb == 2 * g + 1)
                            t_q = None
                            for qs in range(2):
                                if odd_diag and qs == 0:
                                    continue
                                qcol = 0 if odd_diag else qs * 128
                                lw_h = pth[:, psl * GW + qcol:
                                           psl * GW + qcol + 128]
                                lw_l = ptl[:, psl * GW + qcol:
                                           psl * GW + qcol + 128]
                                first = (kb == 0)
                                last = (kb == 2 * g + qs)
                                dep0 = [t_pt[kb]]
                                if first:
                                    dep0 += [t_avev[0], t_avev[1], t_ffne[0],
                                             t_ffne[1], t_den[0], t_den[1],
                                             t_v_done[1], t_ones8]
                                psA = PS[2 + 2 * qs]
                                psB = PS[3 + 2 * qs]
                                psd = PSD[:, qs * 8: qs * 8 + 8]
                                vk = kb * E
                                PE.run(T.matmul, psA, lw_h, vh[:, vk:vk + 512],
                                       start=first, stop=False, deps=dep0,
                                       tok=False)
                                PE.run(T.matmul, psB, lw_h,
                                       vh[:, vk + 512:vk + 1024], start=first,
                                       stop=False, tok=False)
                                PE.run(T.matmul, psd, lw_h, ones8,
                                       start=(kb == 0 and qs == 0), stop=False,
                                       tok=False)
                                PE.run(T.matmul, psA, lw_h, vl[:, vk:vk + 512],
                                       start=False, stop=False, tok=False)
                                PE.run(T.matmul, psB, lw_h,
                                       vl[:, vk + 512:vk + 1024], start=False,
                                       stop=False, tok=False)
                                PE.run(T.matmul, psA, lw_l, vh[:, vk:vk + 512],
                                       start=False, stop=last, tok=False)
                                PE.run(T.matmul, psB, lw_l,
                                       vh[:, vk + 512:vk + 1024], start=False,
                                       stop=last, tok=False)
                                t_q = PE.run(T.matmul, psd, lw_l, ones8,
                                             start=False,
                                             stop=(odd_diag and qs == 1),
                                             tok=True)
                                if last:
                                    t_avdone[qs] = t_q
                            t_av_slot[psl] = t_q

                        st_tile(0)
                        exp_split(0)
                        if kmax > 1:
                            st_tile(1)
                            exp_split(1)
                        av_tile(0)
                        for kb in range(1, kmax):
                            if kb + 1 < kmax:
                                st_tile(kb + 1)
                                exp_split(kb + 1)
                            av_tile(kb)

                        for qs in range(2):
                            c = 2 * g + qs
                            A_sl = agt[:, qs * E:(qs + 1) * E]
                            C_sl = xct[:, qs * E:(qs + 1) * E]
                            T_sl = tgt[:, qs * E:(qs + 1) * E]
                            U_sl = uqt[:, qs * E:(qs + 1) * E]
                            I_sl = indb[:, qs * E:(qs + 1) * E]
                            bq = bnq[:, qs * 12:(qs + 1) * 12]
                            k0, k1, k2, k3 = (col(8 + qs), col(32 + 8 * qs),
                                              col(12 + qs), col(14 + qs))
                            DV.run(V.tensor_copy, k0, PSD[:, qs * 8:qs * 8 + 1],
                                   deps=[t_avdone[0], t_avdone[1]], tok=False)
                            t_rc = DV.run(V.reciprocal, k0, k0)
                            t_den[qs] = t_rc
                            t_a0 = DV.run(V.tensor_scalar_mul, A_sl[:, 0:512],
                                          PS[2 + 2 * qs], k0,
                                          deps=[t_rc, t_avdone[qs], t_epi[qs]])
                            t_a1 = DV.run(V.tensor_scalar_mul, A_sl[:, 512:1024],
                                          PS[3 + 2 * qs], k0)
                            t_avev[qs] = t_a1
                            t_sq2 = AC.run(SC.activation, T_sl, A_sl, AF.Square,
                                           accum_out=col(48 + 8 * qs),
                                           deps=[t_a0, t_a1, t_g4tok[qs],
                                                 t_ffnp[qs]])
                            DV.run(V.reduce_sum, col(26 + 4 * qs), A_sl,
                                   axis=AX.X, deps=[t_a0, t_a1], tok=False)
                            DV.run(V.tensor_scalar_mul,
                                   st[:, 16 + 2 * qs:17 + 2 * qs],
                                   col(26 + 4 * qs), 1.0 / 1024.0, tok=False)
                            DV.run(V.tensor_scalar_mul, col(24 + 4 * qs),
                                   col(48 + 8 * qs), 1.0 / 1024.0,
                                   deps=[t_sq2], tok=False)
                            DV.run(V.tensor_tensor, col(25 + 4 * qs),
                                   st[:, 16 + 2 * qs:17 + 2 * qs],
                                   st[:, 16 + 2 * qs:17 + 2 * qs], ALU.mult,
                                   tok=False)
                            t_ag = DV.run(V.tensor_tensor,
                                          st[:, 17 + 2 * qs:18 + 2 * qs],
                                          col(24 + 4 * qs), col(25 + 4 * qs),
                                          ALU.subtract)
                            AC.run(SC.activation, k1,
                                   st[:, 17 + 2 * qs:18 + 2 * qs], AF.Ln,
                                   bias=c5, deps=[t_ag], tok=False)
                            t_r2 = AC.run(SC.activation, k1, k1, AF.Exp,
                                          scale=-0.5)
                            t_t2 = DV.run(V.tensor_scalar, T_sl, A_sl,
                                          st[:, 16 + 2 * qs:17 + 2 * qs], k1,
                                          ALU.subtract, ALU.mult,
                                          deps=[t_r2, t_sq2])
                            t_C = DV.run(V.tensor_tensor, C_sl, T_sl, C_sl,
                                         ALU.add, deps=[t_t2, t_xd])
                            t_sq3 = AC.run(SC.activation, A_sl, C_sl, AF.Square,
                                           accum_out=col(48 + 8 * qs),
                                           deps=[t_C])
                            DV.run(V.reduce_sum, col(26 + 4 * qs), C_sl,
                                   axis=AX.X, deps=[t_C], tok=False)
                            DV.run(V.tensor_scalar_mul,
                                   st[:, 20 + 2 * qs:21 + 2 * qs],
                                   col(26 + 4 * qs), 1.0 / 1024.0, tok=False)
                            DV.run(V.tensor_scalar_mul, col(24 + 4 * qs),
                                   col(48 + 8 * qs), 1.0 / 1024.0,
                                   deps=[t_sq3], tok=False)
                            DV.run(V.tensor_tensor, col(25 + 4 * qs),
                                   st[:, 20 + 2 * qs:21 + 2 * qs],
                                   st[:, 20 + 2 * qs:21 + 2 * qs], ALU.mult,
                                   tok=False)
                            t_ag3 = DV.run(V.tensor_tensor,
                                           st[:, 21 + 2 * qs:22 + 2 * qs],
                                           col(24 + 4 * qs), col(25 + 4 * qs),
                                           ALU.subtract)
                            m3 = st[:, 20 + 2 * qs:21 + 2 * qs]
                            v3 = st[:, 21 + 2 * qs:22 + 2 * qs]
                            DV.run(V.tensor_scalar_add, k1, v3, 1e-5,
                                   deps=[t_ag3], tok=False)
                            t_arg = DV.run(V.scalar_tensor_tensor, k1, k1, 1e-6,
                                           v3, ALU.mult, ALU.add)
                            AC.run(SC.activation, k1, k1, AF.Ln, deps=[t_arg],
                                   tok=False)
                            t_rho = AC.run(SC.activation, k1, k1, AF.Exp,
                                           scale=-0.5)
                            t_D = DV.run(V.tensor_scalar_sub, A_sl, C_sl, m3,
                                         deps=[t_ag3, t_sq3])
                            t_am = DV.run(V.tensor_reduce, k2, A_sl, axis=AX.X,
                                          op=ALU.max, apply_absolute_value=True)
                            DV.run(V.tensor_tensor, k2, k2, k1, ALU.mult,
                                   deps=[t_rho], tok=False)
                            DV.run(V.tensor_scalar_max, k2, k2, 1e-5, tok=False)
                            DV.run(V.reciprocal, k3, k2, tok=False)
                            t_k = DV.run(V.scalar_tensor_tensor, k3, k3, 127.0,
                                         k1, ALU.mult, ALU.mult)
                            t_ct = DV.run(V.tensor_tensor, k2, k2, ws_bc,
                                          ALU.mult, deps=[t_wsbc])
                            DV.run(V.tensor_scalar, U_sl, A_sl, k3, 127.5,
                                   ALU.mult, ALU.add, deps=[t_k], tok=False)
                            t_u = DV.run(V.tensor_scalar, U_sl, U_sl, 254.5, 0.5,
                                         ALU.min, ALU.max)
                            lv = [128.0, 64.0, 32.0, 16.0, 8.0, 4.0, 2.0, 1.0]
                            src = U_sl
                            dst = T_sl
                            for li, cbit in enumerate(lv):
                                eng = PL if li in (1, 3, 5) else DV
                                fts = G.tensor_scalar if eng is PL \
                                    else V.tensor_scalar
                                ftt = G.tensor_tensor if eng is PL \
                                    else V.tensor_tensor
                                t_iv = eng.run(fts, I_sl, src, cbit, -cbit,
                                               ALU.is_ge, ALU.mult,
                                               deps=[t_u, t_ffnp[qs],
                                                     t_g4tok[qs]])
                                t_u = eng.run(ftt, dst, src, I_sl, ALU.add,
                                              deps=[t_iv])
                                src = dst
                            t_qbn = DV.run(V.scalar_tensor_tensor, I_sl, T_sl,
                                           127.0, U_sl, ALU.add, ALU.subtract,
                                           deps=[t_u])
                            t_tr = None
                            for j in range(EC):
                                t_tr = PE.run(T.transpose,
                                              PTB[:, j * 128:(j + 1) * 128],
                                              I_sl[:, j * 128:(j + 1) * 128],
                                              ident_b,
                                              deps=[t_qbn, t_qb[1 - qs]]
                                              if j == 0 else (),
                                              tok=(j == EC - 1))
                            t_qb[qs] = DV.run(V.tensor_copy, qbt, PTB,
                                              deps=[t_tr,
                                                    t_ffnp[1 - qs]])
                            t_mm = None
                            for h in range(2):
                                ps = PS[2 + 2 * qs + h]
                                for j in range(EC):
                                    dep0 = [t_qb[qs], t_wt, t_avev[qs]] \
                                        if j == 0 else ()
                                    t_mm = PE.run(
                                        T.matmul, ps,
                                        qbt[:, j * 128:(j + 1) * 128],
                                        wt[:, j * E + h * 512:
                                           j * E + h * 512 + 512],
                                        start=(j == 0), stop=(j == EC - 1),
                                        deps=dep0, tok=(j == EC - 1))
                            t_ffnp[qs] = t_mm
                            DV.run(V.scalar_tensor_tensor, T_sl[:, 0:512],
                                   PS[2 + 2 * qs], k2, bias_bc[:, 0:512],
                                   ALU.mult, ALU.add,
                                   deps=[t_mm, t_ct, t_bb0, t_bb1], tok=False)
                            t_f1 = DV.run(V.scalar_tensor_tensor,
                                          T_sl[:, 512:1024], PS[3 + 2 * qs], k2,
                                          bias_bc[:, 512:1024], ALU.mult,
                                          ALU.add)
                            t_ffne[qs] = t_f1
                            t_sq4 = AC.run(SC.activation, A_sl, T_sl, AF.Square,
                                           accum_out=col(48 + 8 * qs),
                                           deps=[t_f1])
                            DV.run(V.reduce_sum, col(26 + 4 * qs), T_sl,
                                   axis=AX.X, deps=[t_f1], tok=False)
                            DV.run(V.tensor_scalar_mul,
                                   st[:, 16 + 2 * qs:17 + 2 * qs],
                                   col(26 + 4 * qs), 1.0 / 1024.0, tok=False)
                            DV.run(V.tensor_scalar_mul, col(24 + 4 * qs),
                                   col(48 + 8 * qs), 1.0 / 1024.0,
                                   deps=[t_sq4], tok=False)
                            DV.run(V.tensor_tensor, col(25 + 4 * qs),
                                   st[:, 16 + 2 * qs:17 + 2 * qs],
                                   st[:, 16 + 2 * qs:17 + 2 * qs], ALU.mult,
                                   tok=False)
                            t_ag4 = DV.run(V.tensor_tensor,
                                           st[:, 17 + 2 * qs:18 + 2 * qs],
                                           col(24 + 4 * qs), col(25 + 4 * qs),
                                           ALU.subtract)
                            AC.run(SC.activation, k1,
                                   st[:, 17 + 2 * qs:18 + 2 * qs], AF.Ln,
                                   bias=c5, deps=[t_ag4], tok=False)
                            t_r4 = AC.run(SC.activation, k1, k1, AF.Exp,
                                          scale=-0.5)
                            t_g4 = DV.run(V.tensor_scalar, A_sl, T_sl,
                                          st[:, 16 + 2 * qs:17 + 2 * qs], k1,
                                          ALU.subtract, ALU.mult,
                                          deps=[t_r4, t_sq4])
                            t_g4tok[qs] = t_g4
                            t_o = DV.run(V.tensor_tensor, A_sl, A_sl, C_sl,
                                         ALU.add, deps=[t_g4])
                            t_out[qs] = t_o
                            QO.dma(out_d[s, c * 128:(c + 1) * 128, :], A_sl,
                                   deps=[t_o])
                            t_epi[qs] = QO.full()
                        t_lq = PE.tok()
                        if g + 1 < NG:
                            t_xd = QX0.dma(
                                xct.rearrange("p (c e) -> p c e", c=2),
                                x_d[s, (g + 1) * 256:(g + 2) * 256, :].rearrange(
                                    "(c p) e -> p c e", c=2),
                                deps=[t_out[0], t_out[1]])
                            QL0.dma(lqh, lnt_d[s * NG + g + 1, :, 0:2048],
                                    deps=[t_lq])
                            t_lqd = QL0.dma(lql,
                                            lnt_d[s * NG + g + 1, :, 2048:4096])

    for q in ALLQ:
        if q.cnt > 0:
            PL.eng.wait_ge(q.sem, q.cnt)
    PL.eng.wait_ge(PE.sem, PE.cnt)
    PL.eng.wait_ge(DV.sem, DV.cnt)
    PL.eng.wait_ge(AC.sem, AC.cnt)
    return nc


_NC_CACHE = {}


def _get_nc(NSEQ, S):
    key = (NSEQ, S)
    if key not in _NC_CACHE:
        _NC_CACHE[key] = build_nc(NSEQ, S)
    return _NC_CACHE[key]


def kernel(L0_attn_x, L0_attn_WQ, L0_attn_WK, L0_attn_WV,
           L0_ffn_w_int8, L0_ffn_weight_scale, L0_ffn_bias):
    x = np.ascontiguousarray(np.asarray(L0_attn_x, dtype=np.float32))
    wq = np.ascontiguousarray(np.asarray(L0_attn_WQ, dtype=np.float32))
    wk = np.ascontiguousarray(np.asarray(L0_attn_WK, dtype=np.float32))
    wv = np.ascontiguousarray(np.asarray(L0_attn_WV, dtype=np.float32))
    wf = np.ascontiguousarray(np.asarray(L0_ffn_w_int8, dtype=np.int32))
    ws = np.asarray(L0_ffn_weight_scale, dtype=np.float32).reshape(1, 1)
    bias = np.asarray(L0_ffn_bias, dtype=np.float32).reshape(1, E)

    Bt, S = x.shape[0], x.shape[1]
    NSEQ = Bt // NCORE
    nc = _get_nc(NSEQ, S)
    in_maps = []
    for c in range(NCORE):
        in_maps.append({
            "x": x[c * NSEQ:(c + 1) * NSEQ],
            "wq": wq, "wk": wk, "wv": wv, "wf": wf,
            "wsc": ws, "bias": bias,
        })
    res = run_bass_kernel_spmd(nc, in_maps, core_ids=list(range(NCORE)))
    out = np.concatenate([r["out"] for r in res.results], axis=0)
    return out.astype(np.float32, copy=False)
